# version 01
# baseline (speedup 1.0000x reference)
import numpy as np
import ml_dtypes
import concourse.bass as bass
import concourse.mybir as mybir
from concourse.bass_utils import run_bass_kernel_spmd

F32 = mybir.dt.float32
BF16 = mybir.dt.bfloat16
I32 = mybir.dt.int32
AF = mybir.ActivationFunctionType
A = mybir.AluOpType
AX = mybir.AxisListType.X

N = 8192
D = 2048
KV = 512
NQH = 16
MLP = 16384
NT = N // 128
NC = 16
SLAB = 512
NSLAB = MLP // SLAB
SCL = 1.0 / np.sqrt(128.0)
C22 = 1 << 22
GELU_C = 0.7978845608028654
GELU_A = 0.044715


def own_tiles(r):
    return [8 * a + r for a in range(8)]


def pi_perm(n):
    p = []
    for b in range(n // 256):
        p += list(range(b * 256, b * 256 + 256, 2))
        p += list(range(b * 256 + 1, b * 256 + 256, 2))
    return np.array(p, dtype=np.int64)


class Sched:
    def __init__(self, nc):
        self.nc = nc
        self.ops = {e: [] for e in ("sync", "vector", "scalar", "tensor", "gpsimd")}
        self.sems = {}
        self.tot = {}
        self.bw = {}
        self.br = {}
        self.engsem = {"vector": "chv", "scalar": "chs", "tensor": "cht",
                       "gpsimd": "chg", "sync": "chy"}
        self.waited = {e: {} for e in self.ops}

    def t_(self, name):
        return self.tot.get(name, 0)

    def op(self, eng, fn, reads=(), writes=(), inc=1, sem=None, extra_wait=()):
        sem = sem or self.engsem[eng]
        need = {}
        for b in reads:
            w = self.bw.get(b)
            if w:
                need[w[0]] = max(need.get(w[0], 0), w[1])
        for b in writes:
            w = self.bw.get(b)
            if w:
                need[w[0]] = max(need.get(w[0], 0), w[1])
            for rs, rt in self.br.get(b, ()):
                need[rs] = max(need.get(rs, 0), rt)
        for s, v in extra_wait:
            need[s] = max(need.get(s, 0), v)
        need.pop(None, None)
        wt = self.waited[eng]
        need = {s: v for s, v in need.items() if v > wt.get(s, 0)}
        for s, v in need.items():
            wt[s] = v
        self.tot[sem] = self.tot.get(sem, 0) + inc
        tgt = self.tot[sem]

        def emit(e, need=need, fn=fn, sem=sem):
            for s, v in need.items():
                if v > 0:
                    e.wait_ge(self.sems[s], v)
            fn(e).then_inc(self.sems[sem], inc)
        self.ops[eng].append(emit)
        for b in reads:
            self.br.setdefault(b, []).append((sem, tgt))
        for b in writes:
            self.bw[b] = (sem, tgt)
            self.br[b] = []
        return tgt

    def dma(self, eng, out, in_, reads=(), writes=(), sem=None, transpose=False,
            extra_wait=()):
        sem = sem or ("qst" if eng == "gpsimd" else "qld")
        if transpose:
            fn = lambda e: e.dma_start_transpose(out=out, in_=in_)
        else:
            fn = lambda e: e.dma_start(out=out, in_=in_)
        return self.op(eng, fn, reads=reads, writes=writes, inc=16, sem=sem,
                       extra_wait=extra_wait)

    def barrier(self):
        snap = [(s, self.tot.get(s, 0)) for s in
                ("chv", "chs", "cht", "chg", "qld", "qld1", "qld2", "qst")]
        for eng in self.ops:
            wt = self.waited[eng]
            todo = [(s, v) for s, v in snap if v > wt.get(s, 0)]
            for s, v in todo:
                wt[s] = v
            def emit(e, todo=todo):
                for s, v in todo:
                    if v > 0:
                        e.wait_ge(self.sems[s], v)
            self.ops[eng].append(emit)
        self.bw = {}
        self.br = {}


def build_core():
    nc = bass.Bass("TRN2", target_bir_lowering=False)

    x = nc.dram_tensor("x", [N, D], F32, kind="ExternalInput").ap()
    xo = nc.dram_tensor("xo", [1024, D], F32, kind="ExternalInput").ap()
    wq = nc.dram_tensor("wq", [D, D], F32, kind="ExternalInput").ap()
    wk = nc.dram_tensor("wk", [D, KV], F32, kind="ExternalInput").ap()
    wv = nc.dram_tensor("wv", [D, KV], F32, kind="ExternalInput").ap()
    wo = nc.dram_tensor("wo", [D, D], F32, kind="ExternalInput").ap()
    gpk = nc.dram_tensor("gpk", [D // 2, MLP], I32, kind="ExternalInput").ap()
    upk = nc.dram_tensor("upk", [D // 2, MLP], I32, kind="ExternalInput").ap()
    dpk = nc.dram_tensor("dpk", [MLP // 2, D], I32, kind="ExternalInput").ap()
    gsc = nc.dram_tensor("gsc", [D // 32, MLP], F32, kind="ExternalInput").ap()
    usc = nc.dram_tensor("usc", [D // 32, MLP], F32, kind="ExternalInput").ap()
    dsc = nc.dram_tensor("dsc", [MLP // 32, D], F32, kind="ExternalInput").ap()
    cosk = nc.dram_tensor("cosk", [N, 64], F32, kind="ExternalInput").ap()
    sink = nc.dram_tensor("sink", [N, 64], F32, kind="ExternalInput").ap()
    cosqo = nc.dram_tensor("cosqo", [1024, 64], F32, kind="ExternalInput").ap()
    sinqo = nc.dram_tensor("sinqo", [1024, 64], F32, kind="ExternalInput").ap()
    identf = nc.dram_tensor("identf", [128, 128], F32, kind="ExternalInput").ap()
    fmsk = nc.dram_tensor("fmsk", [8 * 128, 1024], F32, kind="ExternalInput").ap()
    fmskT = nc.dram_tensor("fmskT", [8 * 128, 1024], F32, kind="ExternalInput").ap()

    y = nc.dram_tensor("y", [1024, D], F32, kind="ExternalOutput").ap()

    xnsp = nc.dram_tensor("xnsp", [N, D], BF16, kind="Internal").ap()
    xnosp = nc.dram_tensor("xnosp", [1024, D], BF16, kind="Internal").ap()
    ksp = nc.dram_tensor("ksp", [N, KV], BF16, kind="Internal").ap()
    vsp = nc.dram_tensor("vsp", [N, KV], BF16, kind="Internal").ap()
    qsp = nc.dram_tensor("qsp", [1024, D], BF16, kind="Internal").ap()
    wdsp = nc.dram_tensor("wdsp", [MLP, D], BF16, kind="Internal").ap()
    itsp = nc.dram_tensor("itsp", [MLP, 1024], BF16, kind="Internal").ap()
    gssp = nc.dram_tensor("gssp", [1024, D], BF16, kind="Internal").ap()
    ctxsp = nc.dram_tensor("ctxsp", [1024, D], BF16, kind="Internal").ap()

    S = Sched(nc)
    import contextlib
    es = contextlib.ExitStack()
    sb = lambda name, shape, dt: es.enter_context(nc.sbuf_tensor(name, shape, dt))

    idf = sb("idf", [128, 128], F32)
    xbuf = [sb(f"xb{i}", [128, D], F32) for i in range(2)]
    xnb = [sb(f"xn{i}", [128, D], BF16) for i in range(2)]
    trg = [sb(f"trg{i}", [128, 128], F32) for i in range(2)]
    xnt = [sb(f"xnt{i}", [128, NC * 512], BF16) for i in range(2)]
    wkb = sb("wkb", [128, NC * 512], BF16)
    wvb = sb("wvb", [128, NC * 512], BF16)
    wqs = [sb(f"wqs{i}", [128, 512], F32) for i in range(2)]
    wqb = wkb
    kst = [sb(f"kst{i}", [128, 512], BF16) for i in range(2)]
    vst = [sb(f"vst{i}", [128, 512], BF16) for i in range(2)]
    rtm = [sb(f"rtm{i}", [128, 256], F32) for i in range(2)]
    st1 = sb("st1", [128, 1], F32)
    st2 = sb("st2", [128, 1], F32)
    st3 = sb("st3", [128, 1], F32)
    st4 = sb("st4", [128, 1], F32)
    pkb = [sb(f"pkb{i}", [128, 512], I32) for i in range(2)]
    scb = [sb(f"scb{i}", [128, 512], F32) for i in range(2)]
    dq_t = [sb(f"dqt{i}", [128, 512], I32) for i in range(2)]
    dq_b = [sb(f"dqb{i}", [128, 512], I32) for i in range(2)]
    dq_s = [sb(f"dqs{i}", [128, 512], I32) for i in range(2)]
    dq_m = [sb(f"dqm{i}", [128, 512], F32) for i in range(2)]
    dqo = [sb(f"dqo{i}", [128, 512], BF16) for i in range(4)]
    gtmp = [sb(f"gt{i}", [128, 512], F32) for i in range(3)]
    itb = sb("itb", [128, 1024], BF16)
    itc = [sb(f"itc{i}", [128, 1024], BF16) for i in range(2)]
    wdc = [sb(f"wdc{i}", [128, 512], BF16) for i in range(2)]
    gsb = [sb(f"gsb{i}", [128, 512], BF16) for i in range(2)]
    qtb = sb("qtb", [128, NQH * 128], BF16)
    ktb = [sb(f"ktb{i}", [128, 4 * 512], BF16) for i in range(2)]
    vtb = [sb(f"vtb{i}", [128, 4 * 4 * 132], BF16) for i in range(2)]
    ptb = [sb(f"ptb{i}", [128, 512], BF16) for i in range(2)]
    mxs = sb("mxs", [128, NQH * 17], F32)
    ctxa = sb("ctxa", [128, NQH * 132], F32)
    bsb = sb("bsb", [128, NQH], F32)
    mjd = sb("mjd", [16, 16], F32)
    ones16 = sb("ones16", [16, 128], F32)
    mjr = sb("mjr", [16, 1], F32)
    linv = sb("linv", [128, NQH], F32)
    ctxo = sb("ctxo", [128, 128], BF16)
    fmb = sb("fmb", [128, 1024], F32)
    fmtb = sb("fmtb", [128, 1024], F32)
    ctb = qtb
    gld = [sb(f"gld{i}", [128, 512], BF16) for i in range(2)]
    xld = [sb(f"xld{i}", [128, 512], F32) for i in range(2)]
    osb = [sb(f"osb{i}", [128, 512], F32) for i in range(2)]

    for name in ("chv", "chs", "cht", "chg", "chy", "qld", "qld1", "qld2", "qst"):
        S.sems[name] = es.enter_context(nc.semaphore(name))

    PA = es.enter_context(nc.psum_tensor("PA", [128, 2048], F32))
    PB = es.enter_context(nc.psum_tensor("PB", [128, 2048], F32))

    def rep_scale(scT, col0, Wn, j0):
        r0 = j0 // 16
        src = scT[r0:r0 + 8, col0:col0 + Wn].partition_broadcast(16)
        return src.rearrange("k r c -> r k c")

    def pi_out(ap2d, W):
        return ap2d.rearrange("p (b q i) -> p b q i", b=W // 256, q=2, i=128)

    def pi_in(ap2d, W):
        return ap2d.rearrange("p (b i q) -> p b q i", b=W // 256, i=128, q=2)

    def dequant(pk_ap, sc_ap, out_lo, out_hi, p, keyin, keyout, W, use_pi):
        t_, b_, s_, m_ = dq_t[p], dq_b[p], dq_s[p], dq_m[p]
        u_ = dq_t[p]
        for hi_ in (False, True):
            out = out_hi if hi_ else out_lo
            ko = (keyout, p, hi_)
            if hi_:
                S.op("vector", lambda e: e.tensor_scalar(
                    out=t_[:, :W], in0=pk_ap, scalar1=4, scalar2=7,
                    op0=A.logical_shift_right, op1=A.bitwise_and),
                    reads=[keyin], writes=[("dt", p)])
                S.op("vector", lambda e: e.tensor_scalar(
                    out=s_[:, :W], in0=pk_ap, scalar1=128, scalar2=24,
                    op0=A.bitwise_and, op1=A.logical_shift_left),
                    reads=[keyin], writes=[("ds", p)])
            else:
                S.op("vector", lambda e: e.tensor_scalar(
                    out=t_[:, :W], in0=pk_ap, scalar1=7, scalar2=None,
                    op0=A.bitwise_and), reads=[keyin], writes=[("dt", p)])
                S.op("vector", lambda e: e.tensor_scalar(
                    out=s_[:, :W], in0=pk_ap, scalar1=8, scalar2=28,
                    op0=A.bitwise_and, op1=A.logical_shift_left),
                    reads=[keyin], writes=[("ds", p)])
            S.op("vector", lambda e: e.tensor_scalar(
                out=b_[:, :W], in0=t_[:, :W], scalar1=252, scalar2=C22,
                op0=A.add, op1=A.mult), reads=[("dt", p)], writes=[("db", p)])
            S.op("vector", lambda e: e.tensor_scalar(
                out=u_.bitcast(F32)[:, :W], in0=b_.bitcast(F32)[:, :W],
                scalar1=-1.0, scalar2=0.0, op0=A.add, op1=A.min),
                reads=[("db", p)], writes=[("dt", p)])
            S.op("gpsimd", lambda e: e.tensor_tensor(
                out=m_[:, :W], in0=b_.bitcast(F32)[:, :W],
                in1=u_.bitcast(F32)[:, :W], op=A.add),
                reads=[("db", p), ("dt", p)], writes=[("dm", p)])
            S.op("vector", lambda e: e.tensor_tensor(
                out=m_.bitcast(I32)[:, :W], in0=m_.bitcast(I32)[:, :W],
                in1=s_[:, :W], op=A.bitwise_or),
                reads=[("dm", p), ("ds", p)], writes=[("dm", p)])
            if use_pi:
                S.op("gpsimd", lambda e, out=out: e.tensor_tensor(
                    out=pi_out(out, W), in0=pi_in(m_[:, :W], W),
                    in1=pi_in(sc_ap, W), op=A.mult),
                    reads=[("dm", p), (keyin, "sc")], writes=[ko])
            else:
                S.op("gpsimd", lambda e, out=out: e.tensor_tensor(
                    out=out, in0=m_[:, :W], in1=sc_ap, op=A.mult),
                    reads=[("dm", p), (keyin, "sc")], writes=[ko])

    S.dma("sync", idf[:], identf[:], writes=["idf"])
    for c in range(NC):
        S.dma("sync", wqs[c % 2][:], wk[c * 128:(c + 1) * 128, :],
              writes=[("wqs", c % 2)])
        S.op("gpsimd", lambda e, c=c: e.tensor_copy(
            out=wkb[:, c * 512:(c + 1) * 512], in_=wqs[c % 2][:]),
            reads=[("wqs", c % 2)], writes=[("wkb", c)])
    for c in range(NC):
        S.dma("sync", wqs[c % 2][:], wv[c * 128:(c + 1) * 128, :],
              writes=[("wqs", c % 2)])
        S.op("gpsimd", lambda e, c=c: e.tensor_copy(
            out=wvb[:, c * 512:(c + 1) * 512], in_=wqs[c % 2][:]),
            reads=[("wqs", c % 2)], writes=[("wvb", c)])

    ddq = [(jt, q) for jt in range(64) for q in range(4)]
    ddq_i = [0]

    def emit_down_dq(nu):
        for _ in range(nu):
            if ddq_i[0] >= len(ddq):
                return
            jt, qq = ddq[ddq_i[0]]
            ddq_i[0] += 1
            p = ddq_i[0] % 2
            j0, col0 = jt * 128, qq * 512
            S.dma("sync", pkb[p][:], dpk[j0:j0 + 128, col0:col0 + 512],
                  writes=[("dpk", p)], sem="qld1")
            S.dma("sync", scb[p][:], rep_scale(dsc, col0, 512, j0),
                  writes=[(("dpk", p), "sc")], sem="qld1")
            dequant(pkb[p][:], scb[p][:], dqo[2 * p][:], dqo[2 * p + 1][:], p,
                    ("dpk", p), "dqd", 512, False)
            S.dma("gpsimd", wdsp[(2 * jt) * 128:(2 * jt + 1) * 128,
                                 col0:col0 + 512], dqo[2 * p][:],
                  reads=[("dqd", p, False)], sem="qst")
            S.dma("gpsimd", wdsp[(2 * jt + 1) * 128:(2 * jt + 2) * 128,
                                 col0:col0 + 512], dqo[2 * p + 1][:],
                  reads=[("dqd", p, True)], sem="qst")

    def ln_tile(src_ap, p, dst_sp_rows):
        S.dma("sync", xbuf[p][:], src_ap, writes=[("x", p)])
        S.op("vector", lambda e, p=p: e.tensor_reduce(
            out=st1[:], in_=xbuf[p][:], axis=AX, op=A.add),
            reads=[("x", p)], writes=["st1"])
        S.op("scalar", lambda e, p=p: e.activation(
            out=xnb[p].bitcast(F32)[:, 0:D // 2], in_=xbuf[p][:, 0:D // 2],
            func=AF.Square, accum_out=st2[:]),
            reads=[("x", p)], writes=["st2", ("xn", p)])
        S.op("scalar", lambda e, p=p: e.activation(
            out=xnb[p].bitcast(F32)[:, 0:D // 2], in_=xbuf[p][:, D // 2:D],
            func=AF.Square, accum_out=st4[:]),
            reads=[("x", p)], writes=["st4", ("xn", p)])
        S.op("vector", lambda e: e.tensor_tensor(
            out=st2[:], in0=st2[:], in1=st4[:], op=A.add),
            reads=["st2", "st4"], writes=["st2"])
        S.op("vector", lambda e: e.tensor_scalar(
            out=st1[:], in0=st1[:], scalar1=1.0 / D, scalar2=None,
            op0=A.mult), reads=["st1"], writes=["st1"])
        S.op("vector", lambda e: e.tensor_scalar(
            out=st3[:], in0=st2[:], scalar1=1.0 / D, scalar2=1e-5,
            op0=A.mult, op1=A.add), reads=["st2"], writes=["st3"])
        S.op("vector", lambda e: e.tensor_tensor(
            out=st4[:], in0=st1[:], in1=st1[:], op=A.mult),
            reads=["st1"], writes=["st4"])
        S.op("vector", lambda e: e.tensor_tensor(
            out=st3[:], in0=st3[:], in1=st4[:], op=A.subtract),
            reads=["st3", "st4"], writes=["st3"])
        S.op("scalar", lambda e: e.activation(
            out=st4[:], in_=st3[:], func=AF.Sqrt),
            reads=["st3"], writes=["st4"])
        S.op("vector", lambda e: e.reciprocal(out=st3[:], in_=st4[:]),
             reads=["st4"], writes=["st3"])
        S.op("vector", lambda e, p=p: e.tensor_scalar(
            out=pi_out(xnb[p][:], D), in0=pi_in(xbuf[p][:], D),
            scalar1=st1[:], scalar2=st3[:], op0=A.subtract, op1=A.mult),
            reads=[("x", p), "st1", "st3"], writes=[("xn", p)])
        S.dma("gpsimd", dst_sp_rows, xnb[p][:], reads=[("xn", p)], sem="qst")

    def fbc(ap2d, n):
        return ap2d.__replace__(ap=[list(ap2d.ap[0]), [0, n],
                                    list(ap2d.ap[1])])

    def rope(psum_ap, trg_t, out_sb, pkey, okey):
        cb = fbc(trg_t[:, 0:64], 4)
        sbr = fbc(trg_t[:, 64:128], 4)
        k4 = psum_ap.rearrange("p (h i q) -> p h i q", h=4, i=64, q=2)
        o4 = out_sb[:].rearrange("p (h i q) -> p h i q", h=4, i=64, q=2)
        r0 = rtm[0][:].rearrange("p (h i) -> p h i", h=4, i=64)
        r1 = rtm[1][:].rearrange("p (h i) -> p h i", h=4, i=64)
        tk = "tg"
        S.op("vector", lambda e: e.tensor_tensor(
            out=r0, in0=k4[:, :, :, 0], in1=cb, op=A.mult),
            reads=[pkey, tk], writes=["r0"])
        S.op("vector", lambda e: e.tensor_tensor(
            out=r1, in0=k4[:, :, :, 1], in1=sbr, op=A.mult),
            reads=[pkey, tk], writes=["r1"])
        S.op("vector", lambda e: e.tensor_tensor(
            out=o4[:, :, :, 0], in0=r0, in1=r1, op=A.subtract),
            reads=["r0", "r1"], writes=[okey])
        S.op("vector", lambda e: e.tensor_tensor(
            out=r0, in0=k4[:, :, :, 0], in1=sbr, op=A.mult),
            reads=[pkey, tk], writes=["r0"])
        S.op("vector", lambda e: e.tensor_tensor(
            out=r1, in0=k4[:, :, :, 1], in1=cb, op=A.mult),
            reads=[pkey, tk], writes=["r1"])
        S.op("vector", lambda e: e.tensor_tensor(
            out=o4[:, :, :, 1], in0=r0, in1=r1, op=A.add),
            reads=["r0", "r1"], writes=[okey])

    for sup in range(16):
        tiles = [4 * sup + i for i in range(4)]
        for tt in tiles:
            ln_tile(x[tt * 128:(tt + 1) * 128, :], tt % 2,
                    xnsp[tt * 128:(tt + 1) * 128, :])
        sp = sup % 2
        qn = S.t_("qst")
        for c in range(NC):
            S.dma("sync", xnt[sp][:, c * 512:(c + 1) * 512],
                  xnsp[sup * 512:(sup + 1) * 512, c * 128:(c + 1) * 128],
                  writes=[("xnt", sp, c)], transpose=True,
                  extra_wait=[("qst", qn)])
        for i, tt in enumerate(tiles):
            p = tt % 2
            kps = PA[:, 0:512] if (tt % 2 == 0) else PA[:, 1536:2048]
            vps = PA[:, 512:1024] if (tt % 2 == 0) else PB[:, 0:512]
            kkey = ("kps", tt % 2)
            vkey = ("vps", tt % 2)

            def mmk(e, i=i, sp=sp, kps=kps):
                for c in range(NC):
                    m = e.matmul(kps, xnt[sp][:, c * 512 + i * 128:
                                              c * 512 + i * 128 + 128],
                                 wkb[:, c * 512:(c + 1) * 512],
                                 start=(c == 0), stop=(c == NC - 1))
                return m
            S.op("tensor", mmk,
                 reads=[("xnt", sp, c) for c in range(NC)] +
                       [("wkb", c) for c in range(NC)], writes=[kkey])
            S.dma("sync", trg[p][:, 0:64], cosk[tt * 128:(tt + 1) * 128, :],
                  writes=["tg"], sem="qld2")
            S.dma("sync", trg[p][:, 64:128], sink[tt * 128:(tt + 1) * 128, :],
                  writes=["tg"], sem="qld2")
            rope(kps, trg[p], kst[p], kkey, ("kst", p))
            S.dma("gpsimd", ksp[tt * 128:(tt + 1) * 128, :], kst[p][:],
                  reads=[("kst", p)], sem="qst")

            def mmv(e, i=i, sp=sp, vps=vps):
                for c in range(NC):
                    m = e.matmul(vps, xnt[sp][:, c * 512 + i * 128:
                                              c * 512 + i * 128 + 128],
                                 wvb[:, c * 512:(c + 1) * 512],
                                 start=(c == 0), stop=(c == NC - 1))
                return m
            S.op("tensor", mmv,
                 reads=[("xnt", sp, c) for c in range(NC)] +
                       [("wvb", c) for c in range(NC)], writes=[vkey])
            S.op("scalar", lambda e, p=p, vps=vps: e.activation(
                out=vst[p][:], in_=vps, func=AF.Identity),
                reads=[vkey], writes=[("vst", p)])
            S.dma("gpsimd", vsp[tt * 128:(tt + 1) * 128, :], vst[p][:],
                  reads=[("vst", p)], sem="qst")
        emit_down_dq(8)
    emit_down_dq(len(ddq))

    S.barrier()

    for oidx in range(8):
        ln_tile(xo[oidx * 128:(oidx + 1) * 128, :], oidx % 2,
                xnosp[oidx * 128:(oidx + 1) * 128, :])
    qn = S.t_("qst")
    for half in range(2):
        for c in range(NC):
            S.dma("sync", xnt[half][:, c * 512:(c + 1) * 512],
                  xnosp[half * 512:(half + 1) * 512, c * 128:(c + 1) * 128],
                  writes=[("xnt", half, c)], transpose=True,
                  extra_wait=[("qst", qn)])
    for eb in range(4):
        for c in range(NC):
            S.dma("sync", wqs[c % 2][:],
                  wq[c * 128:(c + 1) * 128, eb * 512:(eb + 1) * 512],
                  writes=[("wqs", c % 2)])
            S.op("gpsimd", lambda e, c=c: e.tensor_copy(
                out=wqb[:, c * 512:(c + 1) * 512], in_=wqs[c % 2][:]),
                reads=[("wqs", c % 2)], writes=[("wqb", c)])
        for oidx in range(8):
            p = oidx % 2
            half, i = oidx // 4, oidx % 4
            qps = PA[:, 1024:1536] if p == 0 else PB[:, 512:1024]
            qkey = ("qps", p)

            def mmq(e, half=half, i=i, qps=qps):
                for c in range(NC):
                    m = e.matmul(qps, xnt[half][:, c * 512 + i * 128:
                                                c * 512 + i * 128 + 128],
                                 wqb[:, c * 512:(c + 1) * 512],
                                 start=(c == 0), stop=(c == NC - 1))
                return m
            S.op("tensor", mmq,
                 reads=[("xnt", half, c) for c in range(NC)] +
                       [("wqb", c) for c in range(NC)], writes=[qkey])
            S.dma("sync", trg[p][:, 0:64], cosqo[oidx * 128:(oidx + 1) * 128, :],
                  writes=["tg"], sem="qld2")
            S.dma("sync", trg[p][:, 64:128],
                  sinqo[oidx * 128:(oidx + 1) * 128, :],
                  writes=["tg"], sem="qld2")
            rope(qps, trg[p], kst[p], qkey, ("kst", p))
            S.dma("gpsimd", qsp[oidx * 128:(oidx + 1) * 128,
                                eb * 512:(eb + 1) * 512], kst[p][:],
                  reads=[("kst", p)], sem="qst")

    S.barrier()

    wg, wu = wkb, wvb
    for sl in range(NSLAB):
        n0 = sl * SLAB
        for mat, (pk_d, sc_d, wdst, wkey) in enumerate(
                [(gpk, gsc, wg, "wg"), (upk, usc, wu, "wu")]):
            for jt in range(8):
                p = (sl * 16 + mat * 8 + jt) % 2
                j0 = jt * 128
                S.dma("sync", pkb[p][:, 0:512],
                      pk_d[j0:j0 + 128, n0:n0 + SLAB],
                      writes=[("pk", p)], sem="qld1")
                S.dma("sync", scb[p][:, 0:512], rep_scale(sc_d, n0, SLAB, j0),
                      writes=[(("pk", p), "sc")], sem="qld1")
                lo = wdst[:, (2 * jt) * 512:(2 * jt + 1) * 512]
                hi = wdst[:, (2 * jt + 1) * 512:(2 * jt + 2) * 512]
                dequant(pkb[p][:, 0:512], scb[p][:, 0:512], lo, hi, p,
                        ("pk", p), (wkey, jt), 512, True)
        gk = [((("wg", jt)), pp, hh) for jt in range(8) for pp in range(2)
              for hh in (False, True)]
        uk = [((("wu", jt)), pp, hh) for jt in range(8) for pp in range(2)
              for hh in (False, True)]
        for ntile in range(4):
            for half in range(2):
                xk = [("xnt", half, c) for c in range(NC)]
                par = (ntile * 2 + half) % 2
                gps = PA[:, 0:512] if par == 0 else PA[:, 1024:1536]
                ups = PA[:, 512:1024] if par == 0 else PA[:, 1536:2048]
                gkey, ukey = ("gps", par), ("ups", par)

                def mmg(e, ntile=ntile, half=half, gps=gps):
                    for c in range(NC):
                        m = e.matmul(gps, wg[:, c * 512 + ntile * 128:
                                             c * 512 + ntile * 128 + 128],
                                     xnt[half][:, c * 512:(c + 1) * 512],
                                     start=(c == 0), stop=(c == NC - 1))
                    return m
                S.op("tensor", mmg, reads=gk + xk, writes=[gkey])

                def mmu(e, ntile=ntile, half=half, ups=ups):
                    for c in range(NC):
                        m = e.matmul(ups, wu[:, c * 512 + ntile * 128:
                                             c * 512 + ntile * 128 + 128],
                                     xnt[half][:, c * 512:(c + 1) * 512],
                                     start=(c == 0), stop=(c == NC - 1))
                    return m
                S.op("tensor", mmu, reads=uk + xk, writes=[ukey])
                S.op("scalar", lambda e, gps=gps: e.activation(
                    out=gtmp[0][:], in_=gps, func=AF.Square),
                    reads=[gkey], writes=["g0"])
                S.op("vector", lambda e: e.tensor_scalar(
                    out=gtmp[0][:], in0=gtmp[0][:], scalar1=GELU_A,
                    scalar2=1.0, op0=A.mult, op1=A.add),
                    reads=["g0"], writes=["g0"])
                S.op("vector", lambda e, gps=gps: e.tensor_tensor(
                    out=gtmp[1][:], in0=gps, in1=gtmp[0][:], op=A.mult),
                    reads=[gkey, "g0"], writes=["g1"])
                S.op("scalar", lambda e: e.activation(
                    out=gtmp[1][:], in_=gtmp[1][:], func=AF.Tanh,
                    scale=GELU_C), reads=["g1"], writes=["g1"])
                S.op("vector", lambda e, gps=gps: e.scalar_tensor_tensor(
                    out=gtmp[2][:], in0=gtmp[1][:], scalar=1.0, in1=gps,
                    op0=A.add, op1=A.mult),
                    reads=["g1", gkey], writes=["g2"])
                S.op("vector", lambda e, half=half, ups=ups: e.scalar_tensor_tensor(
                    out=itb[:, half * 512:(half + 1) * 512],
                    in0=ups, scalar=0.5, in1=gtmp[2][:],
                    op0=A.mult, op1=A.mult),
                    reads=[ukey, "g2"], writes=[("itb", half)])
            S.dma("gpsimd", itsp[n0 + ntile * 128:n0 + (ntile + 1) * 128, :],
                  itb[:], reads=[("itb", 0), ("itb", 1)], sem="qst")

    S.barrier()

    banks = [PA[:, i * 512:(i + 1) * 512] for i in range(4)] + \
            [PB[:, i * 512:(i + 1) * 512] for i in range(4)]
    for hb in range(4):
        h0 = hb * 512
        for nchunk in range(MLP // 128):
            p = nchunk % 2
            S.dma("sync", itc[p][:], itsp[nchunk * 128:(nchunk + 1) * 128, :],
                  writes=[("itc", p)], sem="qld2")
            S.dma("sync", wdc[p][:],
                  wdsp[nchunk * 128:(nchunk + 1) * 128, h0:h0 + 512],
                  writes=[("wdc", p)], sem="qld2")

            def mmd(e, nchunk=nchunk, p=p):
                for i in range(8):
                    m = e.matmul(banks[i], itc[p][:, i * 128:(i + 1) * 128],
                                 wdc[p][:], start=(nchunk == 0),
                                 stop=(nchunk == MLP // 128 - 1))
                return m
            S.op("tensor", mmd, reads=[("itc", p), ("wdc", p)],
                 writes=[("dps", i) for i in range(8)])
        for i in range(8):
            p = i % 2
            S.op("scalar", lambda e, i=i, p=p: e.activation(
                out=gsb[p][:], in_=banks[i], func=AF.Silu),
                reads=[("dps", j) for j in range(8)], writes=[("gsb", p)])
            S.dma("gpsimd", gssp[i * 128:(i + 1) * 128, h0:h0 + 512],
                  gsb[p][:], reads=[("gsb", p)], sem="qst")

    S.barrier()

    S.dma("sync", idf[:], identf[:], writes=["idf"], sem="qld2")
    S.op("gpsimd", lambda e: e.memset(ones16[:], 1.0), writes=["ones16"])
    for b in range(2):
        S.op("gpsimd", lambda e, b=b: e.memset(vtb[b][:], 1.0),
             writes=[("vtbones", b)])

    for a in range(8):
        nk = 2 * a + 2
        S.dma("sync", fmb[:], fmsk[a * 128:(a + 1) * 128, :],
              writes=["fmb"], sem="qld2")
        S.dma("sync", fmtb[:], fmskT[a * 128:(a + 1) * 128, :],
              writes=["fmtb"], sem="qld2")
        for h in range(NQH):
            S.dma("sync", qtb[:, h * 128:(h + 1) * 128],
                  qsp[a * 128:(a + 1) * 128, h * 128:(h + 1) * 128],
                  writes=[("qtb", h)], transpose=True, sem="qld2")
        for kb in range(nk):
            k0 = kb * 512
            bp = kb % 2
            for h4 in range(4):
                S.dma("sync", ktb[bp][:, h4 * 512:(h4 + 1) * 512],
                      ksp[k0:k0 + 512, h4 * 128:(h4 + 1) * 128],
                      writes=[("ktb", bp, h4)], transpose=True, sem="qld2")
            mi = kb - (nk - 2)
            for h in range(NQH):
                kvh = h % 4
                spS = PA[:, 0:512] if h % 2 == 0 else PA[:, 512:1024]
                skey = ("spS", h % 2)

                def mms(e, h=h, kvh=kvh, bp=bp, spS=spS):
                    return e.matmul(spS, qtb[:, h * 128:(h + 1) * 128],
                                    ktb[bp][:, kvh * 512:(kvh + 1) * 512],
                                    start=True, stop=True)
                S.op("tensor", mms, reads=[("qtb", h), ("ktb", bp, kvh)],
                     writes=[skey])
                if mi >= 0:
                    S.op("vector", lambda e, mi=mi, spS=spS: e.tensor_tensor(
                        out=spS, in0=spS, in1=fmb[:, mi * 512:(mi + 1) * 512],
                        op=A.add), reads=[skey, "fmb"], writes=[skey])
                S.op("vector", lambda e, h=h, kb=kb, spS=spS: e.tensor_reduce(
                    out=mxs[:, h * 17 + kb:h * 17 + kb + 1], in_=spS,
                    axis=AX, op=A.max), reads=[skey], writes=[("mx", h, kb)])
        def redm(e, nk=nk):
            for h in range(NQH):
                m = e.tensor_reduce(out=mxs[:, h * 17 + 16:h * 17 + 17],
                                    in_=mxs[:, h * 17:h * 17 + nk],
                                    axis=AX, op=A.max)
            return m
        S.op("vector", redm,
             reads=[("mx", h, kb) for h in range(NQH) for kb in range(nk)],
             writes=["mcol"])
        mx16 = mxs[:].rearrange("p (h c) -> p h c", h=NQH, c=17)[:, :, 16]
        S.op("vector", lambda e: e.tensor_copy(out=linv[:, 0:NQH], in_=mx16),
             reads=["mcol"], writes=["mxd"])
        S.op("tensor", lambda e: e.transpose(PB[0:16, 0:128],
                                             linv[:, 0:NQH], idf[:]),
             reads=["mxd", "idf"], writes=["spTP"])
        S.op("vector", lambda e: e.tensor_reduce(
            out=mjr[:], in_=PB[0:16, 0:128], axis=AX, op=A.max),
            reads=["spTP"], writes=["mjr"])
        S.op("vector", lambda e: e.tensor_scalar(
            out=mjd[:], in0=idf[0:16, 0:16], scalar1=mjr[:],
            scalar2=-1.0, op0=A.mult, op1=A.mult),
            reads=["mjr", "idf"], writes=["mjd"])
        S.op("tensor", lambda e: e.matmul(
            PB[:, 128:144], ones16[:], mjd[:], start=True, stop=True),
            reads=["ones16", "mjd"], writes=["spB"])
        S.op("vector", lambda e: e.tensor_copy(out=bsb[:], in_=PB[:, 128:144]),
             reads=["spB"], writes=["bsb"])
        S.op("gpsimd", lambda e: e.memset(ctxa[:], 0.0), writes=["ctxa"])
        for kb in range(nk):
            k0 = kb * 512
            bp = kb % 2
            for h4 in range(4):
                S.dma("sync", ktb[bp][:, h4 * 512:(h4 + 1) * 512],
                      ksp[k0:k0 + 512, h4 * 128:(h4 + 1) * 128],
                      writes=[("ktb", bp, h4)], transpose=True, sem="qld2")
            for s_ in range(4):
                dst = vtb[bp][:, s_ * 528:(s_ + 1) * 528].rearrange(
                    "p (h c) -> p h c", h=4, c=132)[:, :, 0:128]
                src = vsp[k0 + s_ * 128:k0 + (s_ + 1) * 128, :].rearrange(
                    "p (h d) -> p h d", h=4, d=128)
                S.dma("sync", dst, src, reads=[("vtbones", bp)],
                      writes=[("vtb", bp, s_)], sem="qld2")
            mi = kb - (nk - 2)
            for h in range(NQH):
                kvh = h % 4
                spST = PA[:, 1024:1536] if h % 2 == 0 else PB[:, 512:1024]
                spPV = PA[:, 1536:2048] if h % 2 == 0 else PB[:, 1024:1536]
                stkey = ("spST", h % 2)
                pvkey = ("spPV", h % 2)

                def mmst(e, h=h, kvh=kvh, bp=bp, spST=spST):
                    for s_ in range(4):
                        m = e.matmul(spST[:, s_ * 128:(s_ + 1) * 128],
                                     ktb[bp][:, kvh * 512 + s_ * 128:
                                             kvh * 512 + (s_ + 1) * 128],
                                     qtb[:, h * 128:(h + 1) * 128],
                                     start=True, stop=True)
                    return m
                S.op("tensor", mmst, reads=[("qtb", h), ("ktb", bp, kvh)],
                     writes=[stkey])
                if mi >= 0:
                    S.op("vector", lambda e, mi=mi, spST=spST: e.tensor_tensor(
                        out=spST, in0=spST,
                        in1=fmtb[:, mi * 512:(mi + 1) * 512], op=A.add),
                        reads=[stkey, "fmtb"], writes=[stkey])
                S.op("scalar", lambda e, h=h, spST=spST: e.activation(
                    out=ptb[h % 2][:], in_=spST, func=AF.Exp,
                    bias=bsb[:, h:h + 1]),
                    reads=[stkey, "bsb"], writes=[("ptb", h % 2)])

                def mmpv(e, h=h, bp=bp, spPV=spPV):
                    for s_ in range(4):
                        m = e.matmul(spPV[:, 0:132],
                                     ptb[h % 2][:, s_ * 128:(s_ + 1) * 128],
                                     vtb[bp][:, s_ * 528 + (h % 4) * 132:
                                              s_ * 528 + (h % 4) * 132 + 132],
                                     start=(s_ == 0), stop=(s_ == 3))
                    return m
                S.op("tensor", mmpv,
                     reads=[("ptb", h % 2)] +
                           [("vtb", bp, s_) for s_ in range(4)] +
                           [("vtbones", bp)],
                     writes=[pvkey])
                S.op("vector", lambda e, h=h, spPV=spPV: e.tensor_tensor(
                    out=ctxa[:, h * 132:(h + 1) * 132], in0=spPV[:, 0:132],
                    in1=ctxa[:, h * 132:(h + 1) * 132], op=A.add),
                    reads=[pvkey, "ctxa"], writes=["ctxa"])
        lcol = ctxa[:].rearrange("p (h c) -> p h c", h=NQH, c=132)[:, :, 128]
        S.op("vector", lambda e, lcol=lcol: e.reciprocal(
            out=linv[:, 0:NQH], in_=lcol), reads=["ctxa"], writes=["linv"])
        for h in range(NQH):
            o = (h % 4) * 4 + h // 4
            S.op("vector", lambda e, h=h: e.tensor_scalar(
                out=ctxo[:], in0=ctxa[:, h * 132:h * 132 + 128],
                scalar1=linv[:, h:h + 1], scalar2=None, op0=A.mult),
                reads=["ctxa", "linv"], writes=["ctxo"])
            S.dma("gpsimd", ctxsp[a * 128:(a + 1) * 128,
                                  o * 128:(o + 1) * 128], ctxo[:],
                  reads=["ctxo"], sem="qst")

    S.barrier()

    wob = wkb
    sps2 = PB[:, 1536:2048]
    for eb in range(4):
        for c in range(NC):
            S.dma("sync", wqs[c % 2][:],
                  wo[c * 128:(c + 1) * 128, eb * 512:(eb + 1) * 512],
                  writes=[("wqs", c % 2)], sem="qld2")
            S.op("gpsimd", lambda e, c=c: e.tensor_copy(
                out=wob[:, c * 512:(c + 1) * 512], in_=wqs[c % 2][:]),
                reads=[("wqs", c % 2)], writes=[("wob", c)])
        for oidx in range(8):
            for c in range(NC):
                S.dma("sync", ctb[:, c * 128:(c + 1) * 128],
                      ctxsp[oidx * 128:(oidx + 1) * 128,
                            c * 128:(c + 1) * 128],
                      writes=[("ctb", c)], transpose=True, sem="qld2")

            def mmo(e):
                for c in range(NC):
                    m = e.matmul(sps2, ctb[:, c * 128:(c + 1) * 128],
                                 wob[:, c * 512:(c + 1) * 512],
                                 start=(c == 0), stop=(c == NC - 1))
                return m
            S.op("tensor", mmo,
                 reads=[("ctb", c) for c in range(NC)] +
                       [("wob", c) for c in range(NC)], writes=["sps2"])
            p = oidx % 2
            S.dma("sync", gld[p][:],
                  gssp[oidx * 128:(oidx + 1) * 128, eb * 512:(eb + 1) * 512],
                  writes=[("gld", p)], sem="qld2")
            S.dma("sync", xld[p][:],
                  xo[oidx * 128:(oidx + 1) * 128, eb * 512:(eb + 1) * 512],
                  writes=[("xld", p)], sem="qld2")
            S.op("vector", lambda e, p=p: e.tensor_tensor(
                out=osb[p][:], in0=sps2, in1=gld[p][:], op=A.mult),
                reads=["sps2", ("gld", p)], writes=[("osb", p)])
            S.op("vector", lambda e, p=p: e.tensor_tensor(
                out=osb[p][:], in0=osb[p][:], in1=xld[p][:], op=A.add),
                reads=[("osb", p), ("xld", p)], writes=[("osb", p)])
            S.dma("gpsimd", y[oidx * 128:(oidx + 1) * 128,
                              eb * 512:(eb + 1) * 512], osb[p][:],
                  reads=[("osb", p)], sem="qst")

    final_qst = S.t_("qst")

    with nc.Block() as block:
        @block.sync
        def _(e):
            for f in S.ops["sync"]:
                f(e)

        @block.vector
        def _(e):
            for f in S.ops["vector"]:
                f(e)

        @block.scalar
        def _(e):
            for f in S.ops["scalar"]:
                f(e)

        @block.tensor
        def _(e):
            for f in S.ops["tensor"]:
                f(e)

        @block.gpsimd
        def _(e):
            for f in S.ops["gpsimd"]:
                f(e)
            e.wait_ge(S.sems["qst"], final_qst)
    es.close()
    return nc


def _masks_for_core(r):
    NEG = -1e9
    fm = np.zeros((8, 128, 1024), np.float32)
    fmt = np.zeros((8, 128, 1024), np.float32)
    for a in range(8):
        j = 8 * a + r
        kbd = j // 4
        nk = 2 * a + 2
        for mi in range(2):
            kb = nk - 2 + mi
            k0 = kb * 512
            blk = np.zeros((128, 512), np.float32)
            if kb > kbd:
                blk[:] = NEG
            elif kb == kbd:
                p = np.arange(128)[:, None]
                c = np.arange(512)[None, :]
                blk = np.where(k0 + c <= j * 128 + p, 0.0, NEG).astype(np.float32)
            fm[a, :, mi * 512:(mi + 1) * 512] = blk
            bt = np.zeros((128, 512), np.float32)
            for s_ in range(4):
                bt[:, s_ * 128:(s_ + 1) * 128] = blk[:, s_ * 128:(s_ + 1) * 128].T
            fmt[a, :, mi * 512:(mi + 1) * 512] = bt
    return fm.reshape(8 * 128, 1024), fmt.reshape(8 * 128, 1024)


def kernel(L0_attn_x, L0_attn_Wq, L0_attn_Wk, L0_attn_Wv, L0_attn_Wo,
           L0_ffn_gate_W_packed, L0_ffn_gate_scales,
           L0_ffn_up_W_packed, L0_ffn_up_scales,
           L0_ffn_down_W_packed, L0_ffn_down_scales):
    x = np.ascontiguousarray(np.asarray(L0_attn_x, np.float32).reshape(N, D))
    PI = pi_perm(D)
    wq = np.ascontiguousarray(np.asarray(L0_attn_Wq, np.float32)[PI, :])
    wk = np.ascontiguousarray(np.asarray(L0_attn_Wk, np.float32)[PI, :])
    wv = np.ascontiguousarray(np.asarray(L0_attn_Wv, np.float32)[PI, :])
    wo = np.ascontiguousarray(np.asarray(L0_attn_Wo, np.float32))
    gpkv = np.ascontiguousarray(np.asarray(L0_ffn_gate_W_packed, np.int32).T)
    upkv = np.ascontiguousarray(np.asarray(L0_ffn_up_W_packed, np.int32).T)
    dpkv = np.ascontiguousarray(np.asarray(L0_ffn_down_W_packed, np.int32).T)
    gscv = np.ascontiguousarray(np.asarray(L0_ffn_gate_scales, np.float32).T)
    uscv = np.ascontiguousarray(np.asarray(L0_ffn_up_scales, np.float32).T)
    dscv = np.ascontiguousarray(np.asarray(L0_ffn_down_scales, np.float32).T)
    inv = 1.0 / (10000.0 ** (np.arange(0, 128, 2, dtype=np.float64) / 128.0))
    ang = np.outer(np.arange(N, dtype=np.float64), inv)
    cosf = np.cos(ang)
    sinf = np.sin(ang)
    coskv = (cosf * SCL).astype(np.float32)
    sinkv = (sinf * SCL).astype(np.float32)
    identf = np.eye(128, dtype=np.float32)

    nc = build_core()
    in_maps = []
    for r in range(8):
        rows = np.array([8 * a + r for a in range(8)])
        ridx = (rows[:, None] * 128 + np.arange(128)[None, :]).reshape(-1)
        fm, fmt = _masks_for_core(r)
        in_maps.append(dict(
            x=x, xo=np.ascontiguousarray(x[ridx]),
            wq=wq, wk=wk, wv=wv, wo=wo,
            gpk=gpkv, upk=upkv, dpk=dpkv, gsc=gscv, usc=uscv, dsc=dscv,
            cosk=coskv, sink=sinkv,
            cosqo=np.ascontiguousarray(cosf[ridx].astype(np.float32)),
            sinqo=np.ascontiguousarray(sinf[ridx].astype(np.float32)),
            identf=identf, fmsk=fm, fmskT=fmt,
        ))
    res = run_bass_kernel_spmd(nc, in_maps, core_ids=list(range(8)))
    out = np.empty((N, D), np.float32)
    for r in range(8):
        yv = res.results[r]["y"]
        for i, tt in enumerate(own_tiles(r)):
            out[tt * 128:(tt + 1) * 128, :] = yv[i * 128:(i + 1) * 128, :]
    return out.reshape(1, N, D)
